# version 01 (final)
# baseline (speedup 1.0000x reference)
import contextlib
import numpy as np
import concourse.bass as bass
import concourse.mybir as mybir
from concourse.bass_utils import run_bass_kernel_spmd

F32 = mybir.dt.float32
F32R = mybir.dt.float32r
BF16 = mybir.dt.bfloat16
AF = mybir.ActivationFunctionType
ALU = mybir.AluOpType

B, S, D = 8, 4096, 1024
C = 16
NSTEPS = C
BSTAGE = 99
JITMODE = "full"
CB = C * B
KT = D // 128
LR = 1e-3
C0 = 2.0 / (B * D)
DBG_XN = False


class Sched:
    COMPUTE = ("pe", "ve", "se", "gp")
    DMACH = ("ldk", "ldv", "cdh", "cdy", "ca", "cst")

    def __init__(self):
        self.ops = {e: [] for e in self.COMPUTE + ("sy_d",)}
        self.counts = {e: 0 for e in self.COMPUTE}
        self.counts.update({f"sy:{c}": 0 for c in self.DMACH})
        self.events = {}

    def op(self, eng, fn, waits=(), event=None, rd=(), wr=(), chan=None):
        ws = []
        for w in waits:
            if w is None:
                continue
            assert w in self.events, f"unknown event {w}"
            ws.append(self.events[w])
        if eng == "sy_d":
            assert chan in self.DMACH, f"sy_d op needs chan, got {chan}"
            key = f"sy:{chan}"
        else:
            key = eng
        self.ops[eng].append((fn, ws, event is not None, tuple(rd), tuple(wr),
                              key))
        if event is not None:
            self.counts[key] += 1 if eng in self.COMPUTE else 16
            assert event not in self.events, f"dup event {event}"
            self.events[event] = (key, self.counts[key])

    def alias(self, new, old):
        assert old in self.events, f"alias target missing {old}"
        self.events[new] = self.events[old]

    def emit(self, eng_name, handle, sems):
        watermark = {}
        for fn, ws, has_event, rd, wr, key in self.ops[eng_name]:
            for (pkey, val) in ws:
                if pkey == key and eng_name != "sy_d":
                    continue
                if pkey == key:
                    continue
                if watermark.get(pkey, 0) >= val:
                    continue
                handle.wait_ge(sems[pkey], val)
                watermark[pkey] = val
            last = fn(handle)
            if has_event:
                assert last is not None
                last.then_inc(sems[key], 1 if eng_name != "sy_d" else 16)


def rr(ap, k=KT):
    return ap.rearrange("p (k x) -> p k x", k=k)


GRID_NP = np.array([0.0, 0.5, 1.0, 1.5, 2.0, 3.0, 4.0, 6.0], np.float32)
MID_NP = np.array([0.25, 0.75, 1.25, 1.75, 2.5, 3.5, 5.0], np.float32)
DI = 2816
NSL = 6
SLW = [512, 512, 512, 512, 512, 256]
SLO = [0, 512, 1024, 1536, 2048, 2560]


def build_scan_program(n_chunks, limit="full", mode="scan", n_own=0):
    nc = bass.Bass("TRN2", target_bir_lowering=False, detect_race_conditions=False)
    sch = Sched()
    op = sch.op
    full = mode == "full"

    w1_0 = nc.dram_tensor("w1_0", [D, D], F32, kind="ExternalInput").ap()
    w2_0 = nc.dram_tensor("w2_0", [D, D], F32, kind="ExternalInput").ap()
    b1_0 = nc.dram_tensor("b1_0", [1, D], F32, kind="ExternalInput").ap()
    b2_0 = nc.dram_tensor("b2_0", [1, D], F32, kind="ExternalInput").ap()
    ident = nc.dram_tensor("ident", [128, 128], F32, kind="ExternalInput").ap()
    masku = nc.dram_tensor("masku", [128, 128], BF16, kind="ExternalInput").ap()
    if full:
        xs = nc.dram_tensor("xs", [n_chunks, 128, D], F32,
                            kind="ExternalInput").ap()
        xo = nc.dram_tensor("xo", [n_own, 128, D], F32,
                            kind="ExternalInput").ap()
        wkv = nc.dram_tensor("wkv", [D, 2 * D], F32, kind="ExternalInput").ap()
        wq_i = nc.dram_tensor("wq", [D, D], F32, kind="ExternalInput").ap()
        wg_i = nc.dram_tensor("wg", [DI, D], F32, kind="ExternalInput").ap()
        wu_i = nc.dram_tensor("wu", [DI, D], F32, kind="ExternalInput").ap()
        wd_i = nc.dram_tensor("wd", [D, DI], F32, kind="ExternalInput").ap()
        wgT_h = nc.dram_tensor("wgT_h", [KT, 22, 128, 128], F32,
                               kind="Internal").ap()
        wuT_h = nc.dram_tensor("wuT_h", [KT, 22, 128, 128], F32,
                               kind="Internal").ap()
        wdT_h = nc.dram_tensor("wdT_h", [22, KT, 128, 128], BF16,
                               kind="Internal").ap()
        xqT_h = nc.dram_tensor("xqT_h", [n_own, 128, D], F32,
                               kind="Internal").ap()
        hq_h = nc.dram_tensor("hq_h", [n_own, 128, DI], BF16,
                              kind="Internal").ap()
        px_h = nc.dram_tensor("px_h", [n_own, 128, D], F32,
                              kind="Internal").ap()
        outp = nc.dram_tensor("outp", [n_own, 128, D], F32,
                              kind="ExternalOutput").ap()
        if DBG_XN:
            dxn = nc.dram_tensor("dxn", [n_own, 128, D], F32,
                                 kind="ExternalOutput").ap()
            dxq = nc.dram_tensor("dxq", [n_own, 128, D], F32,
                                 kind="ExternalOutput").ap()
            dq = nc.dram_tensor("dq", [n_own, 128, D], F32,
                                kind="ExternalOutput").ap()
            dh1 = nc.dram_tensor("dh1", [n_own, 128, D], F32,
                                 kind="ExternalOutput").ap()
            da1 = nc.dram_tensor("da1", [n_own, 128, D], F32,
                                 kind="ExternalOutput").ap()
            dpx = nc.dram_tensor("dpx", [n_own, 128, D], F32,
                                 kind="ExternalOutput").ap()
            dhq_o = nc.dram_tensor("dhq_o", [n_own, 128, DI], BF16,
                                   kind="ExternalOutput").ap()
            dwz = nc.dram_tensor("dwz", [n_own, NSL, 128, 4096], BF16,
                                 kind="ExternalOutput").ap()
            dhd = nc.dram_tensor("dhd", [n_own, NSL, 128, 512], BF16,
                                 kind="ExternalOutput").ap()
            dgt = nc.dram_tensor("dgt", [n_own, 128, DI], F32,
                                 kind="ExternalOutput").ap()
    kv_kind = "Internal" if full else "ExternalInput"
    kT_h = nc.dram_tensor("kT_h", [n_chunks, 128, D], F32, kind=kv_kind).ap()
    kr_h = nc.dram_tensor("kr_h", [n_chunks, 128, D], F32, kind=kv_kind).ap()
    vr_h = nc.dram_tensor("vr_h", [n_chunks, 128, D], F32, kind=kv_kind).ap()
    if not full:
        w1f_o = nc.dram_tensor("w1f", [D, D], F32, kind="ExternalOutput").ap()
        w2f_o = nc.dram_tensor("w2f", [D, D], F32, kind="ExternalOutput").ap()
        bf_o = nc.dram_tensor("bf", [2, D], F32, kind="ExternalOutput").ap()

    ctx = contextlib.ExitStack()

    def sbuf(name, shape, dtype):
        return ctx.enter_context(nc.sbuf_tensor(name, shape, dtype))

    def psum(name, shape, dtype=F32):
        return ctx.enter_context(nc.psum_tensor(name, shape, dtype))

    with ctx:
        W1 = sbuf("W1", [128, KT * D], F32)
        W2 = sbuf("W2", [128, KT * D], F32)
        W2b = sbuf("W2b", [128, KT * D], BF16)
        W2Tb = sbuf("W2Tb", [128, KT * D], BF16)
        Wr = sbuf("Wr", [128, 2 * D], F32R)
        bvec = sbuf("bvec", [1, 2 * D], F32)
        onesf = sbuf("onesf", [1, 128], F32)
        onesb = sbuf("onesb", [1, 128], BF16)
        ones1 = sbuf("ones1", [128, 1], F32R)
        onesfr = sbuf("onesfr", [1, 128], F32R)
        ZB8 = sbuf("ZB8", [128, 8], F32)
        ZB64 = sbuf("ZB64", [128, 64], F32)
        idn = sbuf("idn", [128, 128], F32)
        idc0r = sbuf("idc0r", [128, 128], F32R)
        kTt = sbuf("kTt", [128, D], F32R)
        krow = sbuf("krow", [128, D], F32R)
        vrow = sbuf("vrow", [128, D], F32)
        hDH = sbuf("hDH", [128, D], BF16)
        hDY = sbuf("hDY", [128, D], F32R)
        hA = sbuf("hA", [128, D], F32R)
        hAT = sbuf("hAT", [128, D], BF16)
        hDYT = sbuf("hDYT", [128, D], BF16)
        H0pT = sbuf("H0pT", [128, D], F32)
        A0Tr = sbuf("A0Tr", [128, D], F32R)
        DY0Tr = sbuf("DY0Tr", [128, D], F32R)
        DY0r = sbuf("DY0r", [128, D], F32R)
        DA0r = sbuf("DA0r", [128, D], F32R)
        DH0r = sbuf("DH0r", [128, D], F32R)
        H0row = sbuf("H0row", [128, D], F32)
        A0row = sbuf("A0row", [128, D], F32)
        S0row = sbuf("S0row", [128, D], F32)
        SProw = sbuf("SProw", [128, D], F32)
        G1b = sbuf("G1b", [128, 128], BF16)
        G1u = sbuf("G1u", [128, 128], F32R)
        G1x = sbuf("G1x", [8, 16 * 128], BF16)
        masks = sbuf("masks", [128, 128], BF16)
        idf = sbuf("idf", [128, 128], F32)
        idfr = sbuf("idfr", [128, 128], F32R)
        idb = sbuf("idb", [128, 128], BF16)
        idr = sbuf("idr", [8, 8], F32R)
        A0T = A0Tr[:].bitcast(F32)
        DY0T = DY0Tr[:].bitcast(F32)
        hT = sbuf("hT", [128, 64], F32)
        sT = sbuf("sT", [128, 64], F32)
        aT = sbuf("aT", [128, 64], F32)
        dT = sbuf("dT", [128, 64], BF16)
        spT = sbuf("spT", [128, 64], F32)
        dyTb = sbuf("dyTb", [128, 64], BF16)
        up_r = sbuf("up_r", [128, 8], F32R)
        ct2_r = sbuf("ct2_r", [128, 8], F32R)
        upx = sbuf("upx", [8, 8], F32R)
        ct2x = sbuf("ct2x", [8, 8], F32R)
        sprow = sbuf("sprow", [8, D], F32)
        t2row = sbuf("t2row", [8, D], F32)
        dytq = sbuf("dytq", [8, D], F32R)
        aq = sbuf("aq", [8, D], F32R)
        dhq = sbuf("dhq", [8, D], BF16)
        if full:
            xrow = W2b[:].bitcast(F32)[:, 0:D]
            wst = W2b[:].bitcast(F32)[:, D:2 * D]
            cmpb = hDH[:]
            wtb = S0row[:].bitcast(BF16)[:, D:2 * D]
            wgst = S0row[:].bitcast(F32)[:, 512:1024]
            xnTb = S0row[:].bitcast(BF16)[:, 0:D]
            absb = hAT[:]
            sgnb = hDYT[:]
            accb = H0pT[:]
            sclq = hT[:]
            thrq = sT[:]
            sclh = aT[:]
            fl64 = spT[:]
            s1c = sbuf("s1c", [128, 1], F32)
            s2c = sbuf("s2c", [128, 1], F32)
            uuc = sbuf("uuc", [128, 1], F32)
            u2c = sbuf("u2c", [128, 1], F32)
            rc_ = sbuf("rc_", [128, 1], F32)
            rc2 = sbuf("rc2", [128, 1], F32)
            tc_ = sbuf("tc_", [128, 1], F32)
            tc2 = sbuf("tc2", [128, 1], F32)
            bnc = sbuf("bnc", [128, 1], F32)
        PC = psum("PC", [128, 512])
        PU = psum("PU", [128, 512])
        PY = psum("PY", [128, 512])
        PY2 = psum("PY2", [128, 512])
        PD = psum("PD", [128, 512])
        PD2 = psum("PD2", [128, 512])
        PR = psum("PR", [128, 512])
        PR2 = psum("PR2", [128, 512])

        def init_masters(extra_w=()):
            op("sy_d", lambda e: e.dma_start(
                out=rr(W1[:]), in_=w1_0.rearrange("(k p) n -> p k n", p=128)),
               waits=extra_w, event="ld_w1", chan="ldk")
            op("sy_d", lambda e: e.dma_start(
                out=rr(W2[:]), in_=w2_0.rearrange("(k p) n -> p k n", p=128)),
               event="ld_w2", chan="ldk")
            op("sy_d", lambda e: e.dma_start(out=bvec[0:1, 0:D], in_=b1_0),
               event="ld_b1", chan="ldk")
            op("sy_d", lambda e: e.dma_start(out=bvec[0:1, D:2 * D], in_=b2_0),
               event="ld_b2", chan="ldk")
            fin = sch.counts["sy:ldk"]
            for _nm in ("ld_w1", "ld_w2", "ld_b1", "ld_b2"):
                sch.events[_nm] = ("sy:ldk", fin)
            op("gp", lambda e: e.tensor_copy(W2b[:], W2[:]), waits=("ld_w2",),
               event="w2b-1")
            for a, b_ in (("w1u-1", "ld_w1"), ("w2u-1", "ld_w2"),
                          ("b1u-1", "ld_b1"), ("b2u-1", "ld_b2"),
                          ("y_free0", "ld_w1"), ("d_free0", "ld_w2")):
                sch.alias(a, b_)

        op("sy_d", lambda e: e.dma_start(out=idf[:], in_=ident), event="ld_id", chan="cst")
        op("sy_d", lambda e: e.dma_start(out=masks[:], in_=masku), event="ld_mk", chan="cst")
        for _nm in ("ld_id", "ld_mk"):
            sch.events[_nm] = ("sy:cst", 32)
        op("ve", lambda e: e.tensor_copy(idr[:], idf[0:8, 0:8]),
           waits=("ld_id",), event="idr")
        op("ve", lambda e: e.tensor_copy(idfr[:], idf[:]),
           waits=("ld_id",), event="idfr")
        op("ve", lambda e: e.tensor_copy(idb[:], idf[:]),
           waits=("ld_id",), event="idb")
        op("se", lambda e: e.activation(idn[:], idf[:], AF.Copy, scale=-1.0),
           waits=("ld_id",), event="idn")
        op("gp", lambda e: e.memset(ZB8[:], 0.0), event="zb8")
        op("gp", lambda e: e.memset(ZB64[:], 0.0), event="zb64")
        op("gp", lambda e: e.memset(SProw[:, 0:1], 1.0), event="ones_f")
        op("gp", lambda e: e.memset(onesf[:], 1.0), event="onesf")
        op("ve", lambda e: e.tensor_copy(ones1[:], SProw[:, 0:1]),
           waits=("ones_f",), event="ones")
        def init_z(zw=()):
            op("gp", lambda e: e.memset(S0row[:], 0.0), waits=tuple(zw),
               event="zf")
            op("ve", lambda e: e.tensor_copy(hAT[:],
                                             S0row[:, 0:512].bitcast(BF16)),
               waits=("zf",) + tuple(zw), event="z1")
            op("ve", lambda e: e.tensor_copy(hDYT[:], S0row[:, 0:512].bitcast(BF16)), event="z2")
            op("ve", lambda e: e.tensor_copy(hDH[:], S0row[:, 0:512].bitcast(BF16)), event="z3")
            op("ve", lambda e: e.tensor_copy(hDY[:], S0row[:]), event="z4")
            op("ve", lambda e: e.tensor_copy(hA[:], S0row[:]), event="z5")
            op("ve", lambda e: e.tensor_copy(dhq[:], S0row[0:8, 0:512].bitcast(BF16)), event="z6")
            op("ve", lambda e: e.tensor_copy(dytq[:], S0row[0:8, :]), event="z7")
            op("ve", lambda e: e.tensor_copy(aq[:], S0row[0:8, :]), event="z8")
            op("se", lambda e: e.activation(S0row[:, 0:128], idf[:], AF.Copy,
                                            scale=C0),
               waits=("ld_id", "z8"), event="idc0f")
            op("ve", lambda e: e.tensor_copy(idc0r[:], S0row[:, 0:128]),
               waits=("idc0f",), event="idc0r")
            op("ve", lambda e: e.tensor_copy(onesb[:], onesf[:]),
               waits=("onesf",), event="onesb")
            op("ve", lambda e: e.tensor_copy(onesfr[:], onesf[:]),
               waits=("onesf",), event="onesfr")
            for a, b_ in (("pu_free-1", "z2"), ("pr_free-1", "ld_id"),
                          ("dh-1", "z6"), ("h-1", "z3"),
                          ("dma_dh-1", "z6"), ("dma_dy-1", "z7"),
                          ("dma_a-1", "z8"),
                          ("dma_dh-2", "z6"), ("dma_dy-2", "z7"),
                          ("dma_a-2", "z8"), ("cast_kr-1", "z1")):
                sch.alias(a, b_)
        if full:
            op("gp", lambda e: e.memset(fl64[:], 1e-12), event="fl64")

        def tp_set(src_row, dsts, pref, waits, idap=None):
            evs = []
            for half in range(2):
                bank = PR if half == 0 else PR2
                w0 = tuple(waits) + (tuple(evs[-len(dsts):]) if evs else ())
                for q in range(4):
                    kt = half * 4 + q
                    def f(e, kt=kt, bank=bank):
                        o = bank[:, (kt % 4) * 128:(kt % 4) * 128 + 128]
                        if idap is not None:
                            o = o.bitcast(F32R)
                        return e.transpose(
                            o, src_row[:, kt * 128:(kt + 1) * 128],
                            idf[:] if idap is None else idap)
                    op("pe", f, waits=w0 if q == 0 else (),
                       event=f"{pref}tp{half}" if q == 3 else None,
                       wr=("PR" if half == 0 else "PR2",))
                for di, dst in enumerate(dsts):
                    def g(e, half=half, bank=bank, dst=dst):
                        return e.tensor_copy(dst[:, half * 512:(half + 1) * 512],
                                             bank[:])
                    evn = f"{pref}ev{half}_{di}"
                    op("ve", g, waits=(f"{pref}tp{half}",), event=evn,
                       rd=("PR" if half == 0 else "PR2",))
                    evs.append(evn)
            return evs

        def ln_emit(tag, src, scratch, w0):
            op("se", lambda e: e.activation(scratch[:], src[:], AF.Copy,
                                            accum_out=s1c[:]), waits=w0)
            op("se", lambda e: e.activation(scratch[:], src[:], AF.Square,
                                            accum_out=s2c[:]))
            op("se", lambda e: e.activation(tc_[:], s1c[:], AF.Square,
                                            scale=1.0 / D))
            op("se", lambda e: e.mul(tc_[:], tc_[:], -1.0))
            op("se", lambda e: e.activation(uuc[:], s2c[:], AF.Copy,
                                            scale=1.0 / D, bias=1e-5))
            op("se", lambda e: e.add(uuc[:], uuc[:], tc_[:]))
            op("se", lambda e: e.mul(u2c[:], uuc[:], -0.5))
            op("se", lambda e: e.activation(rc_[:], uuc[:], AF.Copy,
                                            scale=-0.45, bias=1.5))
            cur, nxt = rc_, rc2
            for _ in range(4):
                op("se", lambda e, cur=cur: e.activation(tc2[:], cur[:],
                                                         AF.Square))
                op("se", lambda e: e.activation(tc2[:], tc2[:], AF.Copy,
                                                scale=u2c[:], bias=1.5))
                op("se", lambda e, cur=cur, nxt=nxt: e.mul(nxt[:], tc2[:],
                                                           cur[:]))
                cur, nxt = nxt, cur
            op("se", lambda e: e.mul(tc_[:], s1c[:], -1.0 / D))
            op("se", lambda e, cur=cur: e.mul(bnc[:], tc_[:], cur[:]),
               event=f"ln{tag}")
            return cur

        def xn_emit(tag, src, dst, waits):
            bnb = bnc[:].unsqueeze(2).broadcast_to((128, 1, D))
            def f(e):
                return e.scalar_tensor_tensor(
                    dst[:].rearrange("p (n w) -> p n w", n=1),
                    src[:].rearrange("p (n w) -> p n w", n=1),
                    rc_[:], bnb, ALU.mult, ALU.add)
            op("ve", f, waits=waits, event=f"xn{tag}")

        DGRID = [None, 1.0, 1.0, 1.0, 2.0, 2.0, 4.0]

        def qdq_emit(tag, srcf, W, absf, tmp, w_se=(), w_ve=(), w_acc=(),
                     accoff=0):
            nb = W // 16
            TW = max(W, 512)
            tnb = TW // 16
            op("se", lambda e: e.activation(absf[:, 0:W], srcf, AF.Abs),
               waits=w_se, event=f"qab{tag}")
            op("se", lambda e: e.activation(sgnb[:, 0:W], srcf, AF.Sign),
               event=f"qsg{tag}")
            abT = absf[:, 0:TW].rearrange("p (n w) -> p n w", w=16)
            ab3 = absf[:, 0:W].rearrange("p (n w) -> p n w", w=16)
            t3 = tmp[:, 0:TW // 2].rearrange("p (n w) -> p n w", w=8)
            op("ve", lambda e: e.tensor_tensor(
                t3, abT[:, :, 0:8], abT[:, :, 8:16], ALU.max),
               waits=tuple(w_ve) + (f"qab{tag}",))
            op("ve", lambda e: e.tensor_tensor(
                t3[:, :, 0:4], t3[:, :, 0:4], t3[:, :, 4:8], ALU.max))
            op("ve", lambda e: e.tensor_tensor(
                t3[:, :, 0:2], t3[:, :, 0:2], t3[:, :, 2:4], ALU.max))
            op("ve", lambda e, tnb=tnb: e.tensor_tensor(
                sclq[:, 0:tnb].unsqueeze(2), t3[:, :, 0:1], t3[:, :, 1:2],
                ALU.max), event=f"qrd{tag}")
            op("ve", lambda e: e.scalar_tensor_tensor(
                sclq[:, 0:64], sclq[:, 0:64], 1.0 / 6.0, ZB64[:],
                ALU.mult, ALU.add))
            op("ve", lambda e: e.tensor_tensor(
                sclq[:, 0:64], sclq[:, 0:64], fl64[:], ALU.max),
               waits=("fl64",))
            op("ve", lambda e: e.scalar_tensor_tensor(
                sclh[:, 0:64], sclq[:, 0:64], 0.5, ZB64[:],
                ALU.mult, ALU.add))
            ac3 = accb[:, accoff:accoff + W].rearrange("p (n w) -> p n w",
                                                       w=16)
            cm3 = cmpb[:, 0:W].rearrange("p (n w) -> p n w", w=16)
            for i in range(7):
                def ft(e, i=i):
                    return e.scalar_tensor_tensor(
                        thrq[:, 0:64], sclq[:, 0:64], float(MID_NP[i]),
                        ZB64[:], ALU.mult, ALU.add)
                op("ve", ft)
                thrb = thrq[:, 0:nb].unsqueeze(2).broadcast_to((128, nb, 16))
                if i == 0:
                    op("ve", lambda e, thrb=thrb: e.tensor_tensor(
                        ac3, ab3, thrb, ALU.is_gt),
                       waits=tuple(w_acc))
                else:
                    op("ve", lambda e, thrb=thrb: e.tensor_tensor(
                        cm3, ab3, thrb, ALU.is_gt))
                    op("ve", lambda e, i=i: e.scalar_tensor_tensor(
                        accb[:, accoff:accoff + W], cmpb[:, 0:W], DGRID[i],
                        accb[:, accoff:accoff + W], ALU.mult, ALU.add))
            op("ve", lambda e: e.tensor_tensor(
                accb[:, accoff:accoff + W], accb[:, accoff:accoff + W],
                sgnb[:, 0:W], ALU.mult), waits=(f"qsg{tag}",))
            sclhb = sclh[:, 0:nb].unsqueeze(2).broadcast_to((128, nb, 16))
            op("ve", lambda e, sclhb=sclhb: e.tensor_tensor(
                ac3, ac3, sclhb, ALU.mult), event=f"qdq{tag}")

        def init_wkv():
            WKVb = W1[:].bitcast(BF16)
            prev = None
            for i in range(16):
                kt, half = i // 2, i % 2
                def fd(e, kt=kt, half=half):
                    return e.dma_start(
                        out=H0row[:],
                        in_=wkv[kt * 128:(kt + 1) * 128,
                                half * D:(half + 1) * D])
                op("sy_d", fd, waits=(prev,) if prev else (),
                   event=f"wkvd{i}", chan="ldv")
                def fc(e, kt=kt, half=half):
                    return e.tensor_copy(
                        WKVb[:, kt * 2 * D + half * D:
                             kt * 2 * D + half * D + D], H0row[:])
                op("ve", fc, waits=(f"wkvd{i}",), event=f"wkvc{i}")
                prev = f"wkvc{i}"
            sch.alias("wkvb_cast", prev)

        def pp(m):
            WKVb = W1[:].bitcast(BF16)
            kTtF = W2Tb[:].bitcast(F32)[:, 1024:2048]
            krF = W2Tb[:].bitcast(F32)[:, 2048:3072]
            xw = (f"ppt{m - 1}_tp1",) if m >= 1 else ()
            op("sy_d", lambda e: e.dma_start(out=xrow[:], in_=xs[m]),
               waits=xw, event=f"ppx{m}", chan="ldk")
            ln_emit(f"p{m}", xrow, A0row, (f"ppx{m}",))
            xn_emit(f"p{m}", xrow, xrow, (f"lnp{m}",))
            tpw = (f"xnp{m}",) + (("ld_id",) if m == 0 else ())
            tp_set(xrow, [xnTb], f"ppt{m}_", tpw)
            for kt in range(KT):
                for nh in range(2):
                    def f(e, kt=kt, nh=nh):
                        bank = PY if nh == 0 else PY2
                        return e.matmul(
                            bank[:, 0:512], rr(xnTb[:])[:, kt, :],
                            WKVb[:, kt * 2 * D + nh * 512:
                                 kt * 2 * D + nh * 512 + 512],
                            start=(kt == 0), stop=(kt == KT - 1))
                    ws = ()
                    if kt == 0 and nh == 0:
                        ws = [f"ppt{m}_ev1_0", "wkvb_cast"]
                        if m >= 1:
                            ws.append(f"ppke{m - 1}")
                        ws = tuple(ws)
                    op("pe", f, waits=ws,
                       event=f"ppkm{m}" if kt == KT - 1 and nh == 1 else None,
                       wr=("PY" if nh == 0 else "PY2",))
            for kt in range(KT):
                for nh in range(2):
                    def f(e, kt=kt, nh=nh):
                        bank = PD if nh == 0 else PD2
                        return e.matmul(
                            bank[:, 0:512], rr(xnTb[:])[:, kt, :],
                            WKVb[:, kt * 2 * D + D + nh * 512:
                                 kt * 2 * D + D + nh * 512 + 512],
                            start=(kt == 0), stop=(kt == KT - 1))
                    ws = ()
                    if kt == 0 and nh == 0:
                        ws = (f"ppve{m - 1}",) if m >= 1 else ()
                    op("pe", f, waits=ws,
                       event=f"ppvm{m}" if kt == KT - 1 and nh == 1 else None,
                       wr=("PD" if nh == 0 else "PD2",))
            for nh, bank in ((0, PY), (1, PY2)):
                def f(e, nh=nh, bank=bank):
                    return e.tensor_copy(H0row[:, nh * 512:nh * 512 + 512],
                                         bank[:, 0:512])
                hw = [f"ppkm{m}"]
                if m >= 1:
                    hw.append(f"pptk{m - 1}_tp1")
                op("ve", f, waits=tuple(hw) if nh == 0 else (),
                   event=None, rd=("PY" if nh == 0 else "PY2",))
            for nh, bank in ((0, PY), (1, PY2)):
                def f(e, nh=nh, bank=bank):
                    return e.tensor_copy(krF[:, nh * 512:nh * 512 + 512],
                                         bank[:, 0:512])
                kw = (f"ppwk{m - 1}",) if m >= 1 else ()
                op("ve", f, waits=kw if nh == 0 else (),
                   event=f"ppke{m}" if nh == 1 else None,
                   rd=("PY" if nh == 0 else "PY2",))
            op("sy_d", lambda e: e.dma_start(out=kr_h[m], in_=krF[:]),
               waits=(f"ppke{m}",), event=f"ppwk{m}", chan="cdh")
            for nh, bank in ((0, PD), (1, PD2)):
                def f(e, nh=nh, bank=bank):
                    return e.tensor_copy(vrow[:, nh * 512:nh * 512 + 512],
                                         bank[:, 0:512])
                vw = [f"ppvm{m}"]
                if m >= 1:
                    vw.append(f"ppwv{m - 1}")
                op("ve", f, waits=tuple(vw) if nh == 0 else (),
                   event=f"ppve{m}" if nh == 1 else None,
                   rd=("PD" if nh == 0 else "PD2",))
            op("sy_d", lambda e: e.dma_start(out=vr_h[m], in_=vrow[:]),
               waits=(f"ppve{m}",), event=f"ppwv{m}", chan="cdy")
            ktw = [f"ppke{m}"]
            if m >= 1:
                ktw.append(f"ppwt{m - 1}")
            tp_set(H0row, [kTtF], f"pptk{m}_", tuple(ktw))
            op("sy_d", lambda e: e.dma_start(out=kT_h[m], in_=kTtF[:]),
               waits=(f"pptk{m}_ev1_0",), event=f"ppwt{m}", chan="ca")

        wq_units = ([("g", c) for c in range(22)] +
                    [("u", c) for c in range(22)] +
                    [("d", c, j) for c in range(8) for j in range(3)])
        NWU = len(wq_units)

        def wqdq_unit(ui, m):
            u = wq_units[ui]
            isw = u[0] == "d"
            if not isw:
                src = (wg_i if u[0] == "g" else wu_i)[
                    u[1] * 128:(u[1] + 1) * 128, :]
                W = D
            else:
                _, c, j = u
                W = 1024 if j < 2 else 768
                src = wd_i[c * 128:(c + 1) * 128, j * 1024:j * 1024 + W]
            pw = (f"wq_rd{ui - 1}", f"wq_sg{ui - 1}") if ui >= 1 \
                else ("wkvb_cast",)
            op("sy_d", lambda e: e.dma_start(out=wst[:, 0:W], in_=src),
               waits=pw, event=f"wqd{ui}", chan="ldv")
            w_acc = (f"wqt{ui - 1}_tp_last",) if ui >= 1 else ()
            wse = (f"wqd{ui}",)
            if ui >= 1:
                wse = wse + (f"qdqw{ui - 1}",)
            else:
                wse = wse + ("ones",)
            qdq_emit(f"w{ui}", wst[:, 0:W], W, SProw[:],
                     S0row[:].bitcast(F32)[:, 512:1024],
                     w_se=wse, w_ve=(f"wqd{ui}",), w_acc=w_acc)
            sch.alias(f"wq_rd{ui}", f"qrdw{ui}")
            sch.alias(f"wq_sg{ui}", f"qsgw{ui}")
            nt = W // 128
            for q in range(nt):
                def f(e, q=q):
                    bank = PR if q < 4 else PR2
                    return e.transpose(
                        bank[:, (q % 4) * 128:(q % 4) * 128 + 128],
                        accb[:, q * 128:(q + 1) * 128], idf[:])
                ws = ()
                if q == 0:
                    ws = (f"qdqw{ui}", f"pptk{m}_ev1_0",
                          f"wqs{ui - 1}" if ui >= 1 else "wkvb_cast")
                ev = None
                if q == min(3, nt - 1):
                    ev = f"wqt{ui}_tp0"
                if q == nt - 1:
                    ev = f"wqt{ui}_tp1"
                op("pe", f, waits=ws, event=ev,
                   wr=("PR" if q < 4 else "PR2",))
            sch.alias(f"wqt{ui}_tp_last", f"wqt{ui}_tp1")
            if isw:
                for half in range(2):
                    bank = PR if half == 0 else PR2
                    wdt = min(512, (nt - half * 4) * 128)
                    def fe(e, half=half, bank=bank, wdt=wdt):
                        return e.tensor_copy(
                            wtb[:, half * 512:half * 512 + wdt],
                            bank[:, 0:wdt])
                    op("ve", fe, waits=(f"wqt{ui}_tp{half}",),
                       event=f"wqe{ui}_{half}",
                       rd=("PR" if half == 0 else "PR2",))
                for q in range(nt):
                    dst = wdT_h[u[2] * 8 + q, u[1]]
                    def fd(e, q=q, dst=dst):
                        return e.dma_start(
                            out=dst, in_=wtb[:, q * 128:(q + 1) * 128])
                    ev = f"wqs{ui}" if q == nt - 1 else f"wqsx{ui}_{q}"
                    op("sy_d", fd, waits=(f"wqe{ui}_{0 if q < 4 else 1}",),
                       event=ev, chan="cst")
            else:
                wdst = wgT_h if u[0] == "g" else wuT_h
                for half in range(2):
                    bank = PR if half == 0 else PR2
                    def fe(e, half=half, bank=bank):
                        return e.tensor_copy(wgst[:, 0:512], bank[:, 0:512])
                    hw = [f"wqt{ui}_tp{half}"]
                    if half == 1:
                        hw.append(f"wqsx{ui}_3")
                    op("ve", fe, waits=tuple(hw), event=f"wqe{ui}_{half}",
                       rd=("PR" if half == 0 else "PR2",))
                    for qq in range(4):
                        q = half * 4 + qq
                        dst = wdst[q, u[1]]
                        def fd(e, qq=qq, dst=dst):
                            return e.dma_start(
                                out=dst,
                                in_=wgst[:, qq * 128:(qq + 1) * 128])
                        ev = f"wqs{ui}" if q == nt - 1 else f"wqsx{ui}_{q}"
                        op("sy_d", fd, waits=(f"wqe{ui}_{half}",),
                           event=ev, chan="cst")

        def loads(m):
            stw = (f"h0pt{m - 1}_tp1", f"cast_kr{m - 1}") if m >= 1 else \
                ("ld_w1",)
            if full:
                stw = stw + (f"ppwt{m}",)
                if m == 0:
                    stw = stw + ("z1",)
            op("sy_d", lambda e: e.dma_start(out=H0row[:], in_=kT_h[m]),
               waits=stw, event=f"ld_kt{m}", chan="ldk")
            vw = (f"dy0{m - 1}",) if m >= 1 else ()
            if full:
                vw = vw + (f"ppwv{m}",)
                if m == 0:
                    vw = vw + ("z1", "ld_w1")
            op("sy_d", lambda e: e.dma_start(out=vrow[:], in_=vr_h[m]),
               waits=vw, event=f"ld_vr{m}", chan="ldv")
            kw = (f"h0mm{m - 1}_{KT - 1}", f"g1mm{m - 1}") if m >= 1 else ()
            op("ve", lambda e: e.tensor_copy(kTt[:], H0row[:]),
               waits=(f"ld_kt{m}",) + kw, event=f"cast_kt{m}")
            krw = (f"cast_kt{m}",)
            if full:
                krw = krw + (f"ppwk{m}",)
            op("sy_d", lambda e: e.dma_start(out=H0row[:], in_=kr_h[m]),
               waits=krw, event=f"ld_kr{m}", chan="ldk")
            kw2 = (f"u_done{m - 1}",) if m >= 1 else ()
            op("ve", lambda e: e.tensor_copy(krow[:], H0row[:]),
               waits=(f"ld_kr{m}",) + kw2, event=f"cast_kr{m}")
            sch.alias(f"vadj{m}", f"ld_vr{m}")

        def bases(m):
            t = f"{m}"
            lastp = m * NSTEPS - 1
            for kt in range(KT):
                def f(e, kt=kt):
                    return e.matmul(PC[:, 0:128],
                                    rr(kTt[:])[:, kt, :],
                                    rr(kTt[:])[:, kt, :],
                                    start=(kt == 0), stop=False)
                op("pe", f, waits=(f"cast_kt{m}", f"h{lastp}") if kt == 0 else (),
                   event=None, wr=("PC",))
            op("pe", lambda e: e.matmul(PC[:, 0:128], onesfr[:], onesfr[:],
                                        start=False, stop=True),
               waits=("onesfr",), event=f"g1mm{t}", wr=("PC",))
            op("ve", lambda e: e.tensor_copy(G1b[:], PC[:, 0:128]),
               waits=(f"g1mm{t}",), event=f"g1b{t}", rd=("PC",))
            op("ve", lambda e: e.tensor_tensor(
                G1u[:], PC[:, 0:128], masks[:], ALU.mult),
               waits=(f"g1mm{t}",), event=f"g1u{t}", rd=("PC",))
            if BSTAGE < 1:
                return
            for g4 in range(2):
                for qq in range(8):
                    q = g4 * 8 + qq
                    bank = PR if qq < 4 else PR2
                    def f(e, q=q, qq=qq, bank=bank):
                        return e.matmul(
                            bank[0:8, (qq % 4) * 128:(qq % 4) * 128 + 128],
                            idb[:, 8 * q:8 * q + 8], G1b[:],
                            start=True, stop=True)
                    ws = ()
                    if qq == 0:
                        ws = (f"g1b{t}", "idb")
                        ws += (f"g1x{m}_{g4 - 1}",) if g4 else \
                            ((f"pr_free{lastp}",) if m >= 1 else ("z1",))
                    op("pe", f, waits=ws,
                       event=f"g1xmm{m}_{g4}_{qq // 4}" if qq % 4 == 3 else None,
                       wr=("PR" if qq < 4 else "PR2",))
                for hb, bank in ((0, PR), (1, PR2)):
                    def fe(e, g4=g4, hb=hb, bank=bank):
                        return e.tensor_copy(
                            G1x[0:8, g4 * 1024 + hb * 512:g4 * 1024 + hb * 512 + 512],
                            bank[0:8, :])
                    op("ve", fe, waits=(f"g1xmm{m}_{g4}_{hb}",),
                       event=f"g1x{m}_{g4}" if hb == 1 else None,
                       rd=("PR" if hb == 0 else "PR2",))
            if BSTAGE < 2:
                return
            for kt in range(KT):
                wname = f"wr1_{m}_{kt}"
                def g(e, kt=kt):
                    return e.tensor_copy(Wr[:, (kt % 2) * D:(kt % 2) * D + D],
                                         W1[:, kt * D:(kt + 1) * D])
                gw = [f"w1u{m - 1}"]
                if kt >= 2:
                    gw.append(f"h0mm{m}_{kt - 2}")
                op("gp", g, waits=tuple(gw), event=wname)
                for nh in range(2):
                    def f(e, kt=kt, nh=nh):
                        bank = PY if nh == 0 else PY2
                        return e.matmul(
                            bank[:], rr(kTt[:])[:, kt, :],
                            Wr[:, (kt % 2) * D + nh * 512:(kt % 2) * D + nh * 512 + 512],
                            start=(kt == 0), stop=False)
                    op("pe", f,
                       waits=(wname, f"cast_kt{m}", f"y_free{m}")
                       if nh == 0 else (wname,),
                       event=f"h0mm{m}_{kt}" if nh == 1 and kt < KT - 1
                       else None, wr=("PY" if nh == 0 else "PY2",))
            for nh in range(2):
                def f(e, nh=nh):
                    bank = PY if nh == 0 else PY2
                    return e.matmul(bank[:], onesf[:],
                                    bvec[0:1, nh * 512:nh * 512 + 512],
                                    start=False, stop=True)
                op("pe", f, waits=("onesf", f"b1u{m - 1}") if nh == 0 else (),
                   event=f"h0mm{m}_{KT - 1}" if nh == 1 else None,
                   wr=("PY" if nh == 0 else "PY2",))
            h0stg = f"cast_kr{m}"
            for nh, bank in ((0, PY), (1, PY2)):
                def f(e, nh=nh, bank=bank):
                    return e.tensor_copy(
                        H0row[:, nh * 512:nh * 512 + 512], bank[:])
                op("ve", f, waits=(f"h0mm{m}_{KT - 1}", h0stg),
                   event=f"h0row{t}" if nh == 1 else None,
                   rd=("PY" if nh == 0 else "PY2",))
            op("se", lambda e: e.activation(S0row[:], H0row[:], AF.Sigmoid),
               waits=(f"h0row{t}",), event=f"s0{t}")
            op("ve", lambda e: e.tensor_mul(A0row[:], H0row[:], S0row[:]),
               waits=(f"s0{t}",), event=f"a0{t}")
            op("ve", lambda e: e.tensor_sub(SProw[:], H0row[:], A0row[:]),
               event=f"spraw{t}")
            op("se", lambda e: e.activation(S0row[:], S0row[:], AF.Copy,
                                            scale=C0),
               waits=(f"a0{t}", f"spraw{t}"), event=f"s0c{t}")
            op("ve", lambda e: e.scalar_tensor_tensor(
                SProw[:], SProw[:], 1.0, S0row[:], ALU.add, ALU.mult),
               waits=(f"s0c{t}",), event=f"sp0{t}")
            if BSTAGE < 3:
                return
            tp_set(A0row, [A0Tr], f"a0t{t}_", (f"a0{t}", f"g1x{m}_1"))
            for kt in range(KT):
                wname = f"wr2_{m}_{kt}"
                def g(e, kt=kt):
                    return e.tensor_copy(Wr[:, (kt % 2) * D:(kt % 2) * D + D],
                                         W2[:, kt * D:(kt + 1) * D])
                gw = [f"w2u{m - 1}", f"h0mm{m}_{KT - 1}"]
                if kt >= 2:
                    gw.append(f"y0mm{m}_{kt - 2}")
                op("gp", g, waits=tuple(gw), event=wname)
                for nh in range(2):
                    def f(e, kt=kt, nh=nh):
                        bank = PY if nh == 0 else PY2
                        return e.matmul(
                            bank[:], rr(A0Tr)[:, kt, :],
                            Wr[:, (kt % 2) * D + nh * 512:(kt % 2) * D + nh * 512 + 512],
                            start=(kt == 0), stop=False)
                    op("pe", f,
                       waits=(wname, f"h0row{t}", f"a0t{t}_ev1_0")
                       if nh == 0 else (wname,),
                       event=f"y0mm{m}_{kt}" if nh == 1 and kt < KT - 1 else None,
                       wr=("PY" if nh == 0 else "PY2",))
            for nh in range(2):
                def fv(e, nh=nh):
                    bank = PY if nh == 0 else PY2
                    return e.matmul(bank[:], idn[:],
                                    vrow[:, nh * 512:nh * 512 + 512],
                                    start=False, stop=False)
                op("pe", fv, waits=("idn", f"vadj{m}") if nh == 0 else (),
                   wr=("PY" if nh == 0 else "PY2",))
            for nh in range(2):
                def f(e, nh=nh):
                    bank = PY if nh == 0 else PY2
                    return e.matmul(bank[:], onesf[:],
                                    bvec[0:1, D + nh * 512:D + nh * 512 + 512],
                                    start=False, stop=True)
                op("pe", f, waits=(f"b2u{m - 1}",) if nh == 0 else (),
                   event=f"y0mm{m}_{KT - 1}" if nh == 1 else None,
                   wr=("PY" if nh == 0 else "PY2",))
            for nh, bank in ((0, PY), (1, PY2)):
                def f(e, nh=nh, bank=bank):
                    return e.tensor_copy(
                        DY0r[:, nh * 512:nh * 512 + 512], bank[:])
                op("ve", f, waits=(f"y0mm{m}_{KT - 1}",),
                   event=f"dy0{t}" if nh == 1 else None,
                   rd=("PY" if nh == 0 else "PY2",))
            tp_set(DY0r[:], [DY0Tr], f"dy0t{t}_",
                   (f"dy0{t}", f"a0t{t}_ev1_0", "idfr"), idap=idfr[:])
            if BSTAGE < 4:
                return
            for ktn in range(KT):
                if JITMODE == "notp":
                    def g(e, ktn=ktn):
                        return e.tensor_copy(Wr[:, (ktn % 2) * D:(ktn % 2) * D + D],
                                             W2[:, ktn * D:(ktn + 1) * D])
                    gw = [f"w2u{m - 1}", f"dy0t{t}_ev1_0"]
                    if ktn >= 1:
                        gw += [f"w2tev{m}_{ktn - 1}", f"wrt_rd{m}_{ktn - 1}"]
                    op("gp", g, waits=tuple(gw), event=f"w2ttp{m}_{ktn}")
                    op("ve", lambda e: e.tensor_copy(ct2x[:], idf[0:8, 0:8]),
                       waits=(f"w2ttp{m}_{ktn}",), event=f"wrt{m}_{ktn}")
                    op("se", lambda e: e.copy(upx[:], idf[0:8, 0:8]),
                       waits=(f"w2ttp{m}_{ktn}",), event=f"w2tev{m}_{ktn}")
                    for nh in range(2):
                        def f(e, ktn=ktn, nh=nh):
                            bank = PY if nh == 0 else PY2
                            return e.matmul(
                                bank[:], rr(DY0Tr)[:, ktn, :],
                                Wr[:, (ktn % 2) * D + nh * 512:
                                   (ktn % 2) * D + nh * 512 + 512].bitcast(F32R)
                                if False else
                                Wr[:, (ktn % 2) * D + nh * 512:
                                   (ktn % 2) * D + nh * 512 + 512],
                                start=(ktn == 0), stop=(ktn == KT - 1))
                        op("pe", f,
                           waits=(f"wrt{m}_{ktn}", f"dy0{t}") if nh == 0 else (),
                           event=(f"da0mm{m}_{ktn}" if nh == 1 else None),
                           wr=("PY" if nh == 0 else "PY2",))
                    sch.alias(f"wrt_rd{m}_{ktn}", f"da0mm{m}_{ktn}")
                    continue
                for q in range(KT):
                    def ftp(e, q=q, ktn=ktn):
                        bank = PR if q < 4 else PR2
                        return e.transpose(
                            bank[:, (q % 4) * 128:(q % 4) * 128 + 128],
                            W2[:, q * D + ktn * 128:q * D + ktn * 128 + 128],
                            idf[:])
                    ws = ()
                    if q == 0:
                        ws = (f"w2u{m - 1}", f"dy0t{t}_ev1_0")
                        if ktn >= 1:
                            ws += (f"w2tev{m}_{ktn - 1}", f"wrt_rd{m}_{ktn - 1}")
                    op("pe", ftp, waits=ws,
                       event=f"w2ttp{m}_{ktn}" if q == KT - 1 else None,
                       wr=("PR" if q < 4 else "PR2",))
                for half, bank in ((0, PR), (1, PR2)):
                    def fe(e, half=half, bank=bank, ktn=ktn):
                        return e.tensor_copy(
                            Wr[:, (ktn % 2) * D + half * 512:
                               (ktn % 2) * D + half * 512 + 512], bank[:])
                    op("ve", fe, waits=(f"w2ttp{m}_{ktn}",),
                       event=f"wrt{m}_{ktn}" if half == 1
                       else f"wrtA{m}_{ktn}",
                       rd=("PR" if half == 0 else "PR2",))
                for half, bank in ((0, PR), (1, PR2)):
                    def fb(e, half=half, bank=bank, ktn=ktn):
                        return e.copy(
                            W2Tb[:, ktn * D + half * 512:ktn * D + half * 512 + 512],
                            bank[:])
                    op("se", fb,
                       waits=(f"wrtA{m}_{ktn}",) if half == 0
                       else (f"wrt{m}_{ktn}",),
                       event=f"w2tev{m}_{ktn}" if half == 1 else None,
                       rd=("PR" if half == 0 else "PR2",))
                for nh in range(2):
                    def f(e, ktn=ktn, nh=nh):
                        bank = PY if nh == 0 else PY2
                        return e.matmul(
                            bank[:], rr(DY0Tr)[:, ktn, :],
                            Wr[:, (ktn % 2) * D + nh * 512:
                               (ktn % 2) * D + nh * 512 + 512],
                            start=(ktn == 0), stop=(ktn == KT - 1))
                    op("pe", f,
                       waits=(f"wrt{m}_{ktn}", f"dy0{t}")
                       if nh == 0 else (),
                       event=(f"da0mm{m}_{ktn}" if nh == 1 else None),
                       wr=("PY" if nh == 0 else "PY2",))
                sch.alias(f"wrt_rd{m}_{ktn}", f"da0mm{m}_{ktn}")
            sch.alias(f"w2tbjit{m}", f"w2tev{m}_{KT - 1}")
            for nh, bank in ((0, PY), (1, PY2)):
                def f(e, nh=nh, bank=bank):
                    return e.tensor_copy(DA0r[:, nh * 512:nh * 512 + 512], bank[:])
                op("ve", f, waits=(f"da0mm{m}_{KT - 1}",),
                   event=f"da0{t}" if nh == 1 else None,
                   rd=("PY" if nh == 0 else "PY2",))
            for nh, bank in ((0, PY), (1, PY2)):
                def f2(e, nh=nh, bank=bank):
                    return e.tensor_mul(DH0r[:, nh * 512:nh * 512 + 512], bank[:],
                                        SProw[:, nh * 512:nh * 512 + 512])
                op("ve", f2, waits=(f"sp0{t}",),
                   event=f"dh0{t}" if nh == 1 else None,
                   rd=("PY" if nh == 0 else "PY2",))
            if BSTAGE < 5:
                return
            for nh in range(2):
                def f(e, nh=nh):
                    bank = PD if nh == 0 else PD2
                    return e.matmul(bank[:], G1u[:],
                                    DH0r[:, nh * 512:nh * 512 + 512],
                                    start=True, stop=True)
                op("pe", f, waits=(f"g1u{t}", f"dh0{t}", f"d_free{m}")
                   if nh == 0 else (),
                   event=f"c10{t}" if nh == 1 else None,
                   wr=("PD" if nh == 0 else "PD2",))
            for nh, bank in ((0, PD), (1, PD2)):
                def f(e, nh=nh, bank=bank):
                    return e.scalar_tensor_tensor(
                        H0row[:, nh * 512:nh * 512 + 512], bank[:], -LR,
                        H0row[:, nh * 512:nh * 512 + 512], ALU.mult, ALU.add)
                op("ve", f, waits=(f"c10{t}",),
                   event=f"h0p{t}" if nh == 1 else None,
                   rd=("PD" if nh == 0 else "PD2",))
            tp_set(H0row, [H0pT], f"h0pt{t}_",
                   (f"h0p{t}", f"dy0t{t}_ev1_0", f"w2tev{m}_{KT - 1}"))

        def step(m, r):
            t = m * NSTEPS + r
            J = 8 * r
            Je = J - 8
            if r > 0:
                for kt in range(KT):
                    def fl(e, kt=kt):
                        return e.matmul(
                            PC[:, 64 + kt * 8:64 + kt * 8 + 8],
                            dhq[0:8, kt * 128:(kt + 1) * 128],
                            G1x[0:8, (r - 1) * 128 + J:(r - 1) * 128 + J + 8],
                            start=True, stop=(Je == 0))
                    op("pe", fl,
                       waits=(f"dh{t - 1}", f"g1x{m}_1", f"h{t - 1}",
                              f"dma_dh{t - 2}", f"g1b{m}")
                       if kt == 0 else (),
                       event=(f"pc{t}" if Je == 0 and kt == KT - 1 else None),
                       wr=("PC",))
                    if Je > 0:
                        def f(e, kt=kt):
                            return e.matmul(
                                PC[:, 64 + kt * 8:64 + kt * 8 + 8],
                                hDH[0:Je, kt * 128:(kt + 1) * 128],
                                G1b[0:Je, J:J + 8], start=False, stop=True)
                        op("pe", f, waits=(),
                           event=f"pc{t}" if kt == KT - 1 else None, wr=("PC",))
                def fh(e):
                    return e.scalar_tensor_tensor(
                        rr(hT[:], KT), rr(PC[:, 64:128], KT), -LR,
                        rr(H0pT)[:, :, J:J + 8], ALU.mult, ALU.add)
                op("ve", fh, waits=(f"pc{t}", f"h0pt{m}_ev1_0"), event=f"h{t}",
                   rd=("PC",))
            else:
                def fh(e):
                    return e.tensor_copy(rr(hT[:], KT), rr(H0pT)[:, :, J:J + 8])
                op("ve", fh, waits=(f"h0pt{m}_ev1_0",), event=f"h{t}")
            op("se", lambda e: e.activation(sT[:], hT[:], AF.Sigmoid),
               waits=(f"h{t}",), event=f"s{t}")
            op("ve", lambda e: e.tensor_mul(aT[:], hT[:], sT[:]),
               waits=(f"s{t}",), event=f"a{t}")
            def f3(e):
                return e.tensor_copy(rr(hAT)[:, :, J:J + 8], rr(aT[:], KT))
            op("ve", f3, waits=(f"u_read{t - 1}" if t > 0 else None,),
               event=f"aTapp{t}")
            def f4(e):
                return e.tensor_sub(rr(dT[:], KT), rr(aT[:], KT),
                                    rr(A0T)[:, :, J:J + 8])
            op("ve", f4, waits=(f"a0t{m}_ev1_0",), event=f"d{t}")
            op("ve", lambda e: e.tensor_sub(spT[:], hT[:], aT[:]))
            op("ve", lambda e: e.scalar_tensor_tensor(
                spT[:], spT[:], 1.0, sT[:], ALU.add, ALU.mult), event=f"sp{t}")
            if r > 0:
                for kt in range(KT):
                    def f(e, kt=kt):
                        return e.matmul(PU[:, 0:8],
                                        hAT[:, kt * 128:(kt + 1) * 128],
                                        rr(hAT)[:, kt, J:J + 8],
                                        start=(kt == 0), stop=False)
                    op("pe", f, waits=(f"aTapp{t}", f"pu_free{t - 1}")
                       if kt == 0 else (),
                       event=None, wr=("PU",))
                op("pe", lambda e: e.matmul(PU[:, 0:8], onesb[0:1, :],
                                            onesb[0:1, 0:8],
                                            start=False, stop=True),
                   waits=("onesb",), event=f"pu{t}", wr=("PU",))
                op("ve", lambda e: e.scalar_tensor_tensor(
                    up_r[:], PU[:, 0:8], -LR * C0, ZB8[:], ALU.mult, ALU.add),
                   waits=(f"pu{t}", "zb8"), event=f"up{t}", rd=("PU",))
                def fx(e):
                    return e.matmul(PC[0:8, 0:8], idfr[:, max(Je, 0):max(Je, 0) + 8],
                                    up_r[:], start=True, stop=True)
                op("pe", fx, waits=(f"up{t}", f"h{t}"), event=f"upxmm{t}",
                   wr=("PC",))
                op("ve", lambda e: e.tensor_copy(upx[:], PC[0:8, 0:8]),
                   waits=(f"upxmm{t}",), event=f"upx{t}", rd=("PC",))
            for kt in range(KT):
                for nh in range(2):
                    def f(e, kt=kt, nh=nh):
                        bank = PY if nh == 0 else PY2
                        return e.matmul(
                            bank[0:8, :], dT[:, kt * 8:(kt + 1) * 8],
                            W2b[:, kt * D + nh * 512:kt * D + nh * 512 + 512],
                            start=(kt == 0),
                            stop=(r == 0 and kt == KT - 1))
                    ws = ()
                    if kt == 0 and nh == 0:
                        ws = (f"d{t}", f"w2b{m - 1}",
                              f"y_read{t - 1}" if r > 0 else f"dh0{m}")
                    op("pe", f, waits=ws,
                       event=f"py{t}" if r == 0 and kt == KT - 1 and nh == 1
                       else None, wr=("PY" if nh == 0 else "PY2",))
            if r > 0:
                for nh in range(2):
                    def f(e, nh=nh):
                        bank = PY if nh == 0 else PY2
                        return e.matmul(
                            bank[0:8, :], upx[:],
                            dytq[0:8, nh * 512:nh * 512 + 512],
                            start=False, stop=False)
                    op("pe", f, waits=(f"upx{t}",) if nh == 0 else (),
                       wr=("PY" if nh == 0 else "PY2",))
                for nh in range(2):
                    def f(e, nh=nh):
                        bank = PY if nh == 0 else PY2
                        return e.matmul(
                            bank[0:8, :], up_r[0:J, :],
                            DY0r[0:J, nh * 512:nh * 512 + 512],
                            start=False, stop=(Je == 0))
                    op("pe", f, waits=(f"dy0{m}",) if nh == 0 else (),
                       event=f"py{t}" if Je == 0 and nh == 1 else None,
                       wr=("PY" if nh == 0 else "PY2",))
                if Je > 0:
                    for nh in range(2):
                        def f(e, nh=nh):
                            bank = PY if nh == 0 else PY2
                            return e.matmul(
                                bank[0:8, :], up_r[0:Je, :],
                                hDY[0:Je, nh * 512:nh * 512 + 512],
                                start=False, stop=True)
                        op("pe", f, waits=(f"dma_dy{t - 2}",) if nh == 0 else (),
                           event=f"py{t}" if nh == 1 else None,
                           wr=("PY" if nh == 0 else "PY2",))
            for nh, bank in ((0, PY), (1, PY2)):
                def f(e, nh=nh, bank=bank):
                    return e.tensor_copy(dytq[0:8, nh * 512:nh * 512 + 512],
                                         bank[0:8, :])
                op("ve", f, waits=(f"py{t}", f"dma_dy{t - 1}")
                   if nh == 0 else (),
                   event=f"dyev{t}" if nh == 1 else None,
                   rd=("PY" if nh == 0 else "PY2",))
            op("sy_d", lambda e: e.dma_start(out=hDY[J:J + 8, :], in_=dytq[0:8, :]),
               waits=(f"dyev{t}",), event=f"dma_dy{t}", chan="cdy")
            for kt in range(KT):
                def f(e, kt=kt):
                    return e.transpose(
                        PU[:, 64 + kt * 8:64 + kt * 8 + 8].bitcast(F32R),
                        dytq[0:8, kt * 128:(kt + 1) * 128], idr[:])
                ws = ()
                if kt == 0:
                    ws = (f"dyev{t}",)
                    ws += (f"up{t}", f"upx{t}") if r > 0 else (f"pu_free{t - 1}",)
                op("pe", f, waits=ws, event=f"pt{t}" if kt == KT - 1 else None,
                   wr=("PU",))
            op("ve", lambda e: e.scalar_tensor_tensor(
                dyTb[:], PU[:, 64:128], C0, ZB64[:], ALU.mult, ALU.add),
               waits=(f"pt{t}", "zb64"), event=f"dyTb{t}", rd=("PU",))
            def f7(e):
                return e.scalar_tensor_tensor(
                    rr(hDYT)[:, :, J:J + 8], rr(PU[:, 64:128], KT), 1.0,
                    rr(DY0T)[:, :, J:J + 8], ALU.mult, ALU.add)
            op("ve", f7, waits=(f"dy0t{m}_ev1_0",), event=f"dyTapp{t}",
               rd=("PU",))
            if r > 0:
                for kt in range(KT):
                    def f(e, kt=kt):
                        return e.matmul(PU[:, 0:8],
                                        hDYT[:, kt * 128:(kt + 1) * 128],
                                        rr(hDYT)[:, kt, J:J + 8],
                                        start=(kt == 0), stop=(kt == KT - 1))
                    op("pe", f, waits=(f"dyTapp{t}", f"dyTb{t}")
                       if kt == 0 else (),
                       event=f"p5{t}" if kt == KT - 1 else None, wr=("PU",))
                op("ve", lambda e: e.scalar_tensor_tensor(
                    ct2_r[:], PU[:, 0:8], -LR * C0 * C0, ZB8[:],
                    ALU.mult, ALU.add),
                   waits=(f"p5{t}", "zb8"), event=f"ct2{t}", rd=("PU",))
                def fx2(e):
                    return e.matmul(PC[0:8, 8:16],
                                    idfr[:, max(Je, 0):max(Je, 0) + 8],
                                    ct2_r[:], start=True, stop=True)
                op("pe", fx2, waits=(f"ct2{t}", f"upx{t}"), event=f"ct2xmm{t}",
                   wr=("PC",))
                op("ve", lambda e: e.tensor_copy(ct2x[:], PC[0:8, 8:16]),
                   waits=(f"ct2xmm{t}",), event=f"ct2x{t}", rd=("PC",))
            for kt in range(KT):
                for nh in range(2):
                    def f(e, kt=kt, nh=nh):
                        bank = PD if nh == 0 else PD2
                        return e.matmul(
                            bank[0:8, :], dyTb[:, kt * 8:(kt + 1) * 8],
                            W2Tb[:, kt * D + nh * 512:kt * D + nh * 512 + 512],
                            start=(kt == 0), stop=False)
                    ws = ()
                    if kt == 0 and nh == 0:
                        ws = (f"dyTb{t}", f"w2tbjit{m}",
                              f"d_read{t - 1}" if r > 0 else f"h0p{m}")
                    op("pe", f, waits=ws, wr=("PD" if nh == 0 else "PD2",))
            if r > 0:
                for nh in range(2):
                    def f(e, nh=nh):
                        bank = PD if nh == 0 else PD2
                        return e.matmul(bank[0:8, :], ct2x[:],
                                        aq[0:8, nh * 512:nh * 512 + 512],
                                        start=False, stop=False)
                    op("pe", f, waits=(f"ct2x{t}",) if nh == 0 else (),
                       wr=("PD" if nh == 0 else "PD2",))
                if Je > 0:
                    for nh in range(2):
                        def f(e, nh=nh):
                            bank = PD if nh == 0 else PD2
                            return e.matmul(bank[0:8, :], ct2_r[0:Je, :],
                                            hA[0:Je, nh * 512:nh * 512 + 512],
                                            start=False, stop=False)
                        op("pe", f, waits=(f"dma_a{t - 2}",) if nh == 0 else (),
                           wr=("PD" if nh == 0 else "PD2",))
            for nh in range(2):
                def f(e, nh=nh):
                    bank = PD if nh == 0 else PD2
                    return e.matmul(bank[0:8, :], idc0r[:, J:J + 8],
                                    DA0r[:, nh * 512:nh * 512 + 512],
                                    start=False, stop=True)
                op("pe", f, waits=(f"da0{m}", "idc0r") if nh == 0 else (),
                   event=f"pd{t}" if nh == 1 else None,
                   wr=("PD" if nh == 0 else "PD2",))
            for kt in range(KT):
                def f(e, kt=kt):
                    bank = PR if kt < 4 else PR2
                    return e.transpose(
                        bank[0:8, (kt % 4) * 128:(kt % 4) * 128 + 128],
                        spT[:, kt * 8:(kt + 1) * 8], idf[:])
                op("pe", f, waits=(f"sp{t}", f"pr_free{t - 1}",
                                   f"h0pt{m}_ev1_0") if kt == 0 else (),
                   event=f"p7{t}" if kt == KT - 1 else None,
                   wr=("PR" if kt < 4 else "PR2",))
            for nh, bank in ((0, PR), (1, PR2)):
                def f(e, nh=nh, bank=bank):
                    return e.activation(sprow[0:8, nh * 512:nh * 512 + 512],
                                        bank[0:8, :], AF.Copy)
                op("se", f, waits=(f"p7{t}",),
                   event=f"sprow{t}" if nh == 1 else None,
                   rd=("PR" if nh == 0 else "PR2",))
            for kt in range(KT):
                def f(e, kt=kt):
                    bank = PR if kt < 4 else PR2
                    return e.transpose(
                        bank[0:8, (kt % 4) * 128:(kt % 4) * 128 + 128],
                        aT[:, kt * 8:(kt + 1) * 8], idf[:])
                op("pe", f, waits=(f"sprow{t}", f"a{t}") if kt == 0 else (),
                   event=f"p8{t}" if kt == KT - 1 else None,
                   wr=("PR" if kt < 4 else "PR2",))
            for nh, bank in ((0, PR), (1, PR2)):
                def f(e, nh=nh, bank=bank):
                    return e.tensor_copy(aq[0:8, nh * 512:nh * 512 + 512],
                                         bank[0:8, :])
                op("ve", f, waits=(f"p8{t}", f"dma_a{t - 1}")
                   if nh == 0 else (),
                   event=f"aApp{t}" if nh == 1 else None,
                   rd=("PR" if nh == 0 else "PR2",))
            op("sy_d", lambda e: e.dma_start(out=hA[J:J + 8, :], in_=aq[0:8, :]),
               waits=(f"aApp{t}",), event=f"dma_a{t}", chan="ca")
            for nh in range(2):
                def f(e, nh=nh):
                    bank = PR if nh == 0 else PR2
                    return e.matmul(bank[0:8, :], idfr[:, J:J + 8],
                                    DH0r[:, nh * 512:nh * 512 + 512],
                                    start=True, stop=True)
                op("pe", f, waits=(f"aApp{t}", f"dh0{m}") if nh == 0 else (),
                   event=f"dh0x{t}" if nh == 1 else None,
                   wr=("PR" if nh == 0 else "PR2",))
            for nh, bank in ((0, PD), (1, PD2)):
                def f(e, nh=nh, bank=bank):
                    return e.tensor_mul(t2row[0:8, nh * 512:nh * 512 + 512],
                                        bank[0:8, :],
                                        sprow[0:8, nh * 512:nh * 512 + 512])
                op("ve", f, waits=(f"pd{t}", f"sprow{t}") if nh == 0 else (),
                   event=f"t2{t}" if nh == 1 else None,
                   rd=("PD" if nh == 0 else "PD2",))
            for nh, bank in ((0, PR), (1, PR2)):
                def f(e, nh=nh, bank=bank):
                    return e.tensor_sub(dhq[0:8, nh * 512:nh * 512 + 512],
                                        t2row[0:8, nh * 512:nh * 512 + 512],
                                        bank[0:8, :])
                op("ve", f, waits=(f"t2{t}", f"dh0x{t}", f"dma_dh{t - 1}")
                   if nh == 0 else (),
                   event=f"dh{t}" if nh == 1 else None,
                   rd=("PR" if nh == 0 else "PR2",))
            op("sy_d", lambda e: e.dma_start(out=hDH[J:J + 8, :], in_=dhq[0:8, :]),
               waits=(f"dh{t}",), event=f"dma_dh{t}", chan="cdh")
            sch.alias(f"y_read{t}", f"dyev{t}")
            sch.alias(f"d_read{t}", f"t2{t}")
            sch.alias(f"pr_free{t}", f"dh{t}")
            sch.alias(f"u_read{t}", f"pu{t}" if r > 0 else f"aTapp{t}")
            sch.alias(f"pu_free{t}", f"ct2{t}" if r > 0 else f"dyTapp{t}")

        def updates(m):
            t = f"{m}"
            last = m * NSTEPS + NSTEPS - 1
            dhtr = DH0r[:]
            dytr = DY0r[:]
            op("ve", lambda e: e.tensor_tensor(dhtr, DH0r[:], hDH[:], ALU.add),
               waits=(f"dh{last}", f"dma_dh{last}", f"dh0x{last}"),
               event=f"dhtr{t}")
            op("ve", lambda e: e.tensor_tensor(dytr, DY0r[:], hDY[:], ALU.add),
               waits=(f"dma_dy{last}", f"py{last}"), event=f"dytr{t}")

            def upd(name, lhs_ap, lhs_base, rhs, Wm, wev, extra_w=(),
                    scal=-LR):
                banks = [(PY, PY2), (PD, PD2)]
                ev_last = None
                for mt in range(KT):
                    bpair = banks[mt % 2]
                    for nh in range(2):
                        def f(e, mt=mt, nh=nh, bpair=bpair):
                            return e.matmul(
                                bpair[nh][:],
                                lhs_ap[:, lhs_base + mt * 128:
                                       lhs_base + mt * 128 + 128],
                                rhs[:, nh * 512:nh * 512 + 512],
                                start=True, stop=True)
                        ws = [wev] + list(extra_w)
                        if mt >= 2:
                            ws.append(f"{name}ev{mt - 2}_{t}")
                        op("pe", f, waits=tuple(ws) if nh == 0 else (),
                           event=f"{name}mm{mt}_{t}" if nh == 1 else None,
                           wr=(("PY", "PY2", "PD", "PD2")[(mt % 2) * 2 + nh],))
                    for nh in range(2):
                        def g(e, mt=mt, nh=nh, bpair=bpair):
                            return e.scalar_tensor_tensor(
                                Wm[:, mt * D + nh * 512:mt * D + nh * 512 + 512],
                                bpair[nh][:], scal,
                                Wm[:, mt * D + nh * 512:mt * D + nh * 512 + 512],
                                ALU.mult, ALU.add)
                        op("ve", g, waits=(f"{name}mm{mt}_{t}",),
                           event=f"{name}ev{mt}_{t}" if nh == 1 else None,
                           rd=(("PY", "PY2", "PD", "PD2")[(mt % 2) * 2 + nh],))
                    ev_last = f"{name}ev{mt}_{t}"
                return ev_last

            e1 = upd("uw1", krow[:], 0, dhtr, W1, f"dhtr{t}",
                     (f"d_read{last}", f"y_read{last}"))
            e2 = upd("uw2", hA[:], 0, dytr, W2, f"dytr{t}",
                     (f"uw1ev{KT - 2}_{t}", e1, f"dma_a{last}"), scal=-LR * C0)
            sch.alias(f"w1u{m}", e1)
            sch.alias(f"w2u{m}", e2)
            for which, hist, boff, wv in (("b1", dhtr, 0, f"dhtr{t}"),
                                          ("b2", dytr, D, f"dytr{t}")):
                for nh in range(2):
                    def f(e, which=which, hist=hist, nh=nh):
                        bank = PY if nh == 0 else PY2
                        return e.matmul(bank[0:1, :], ones1[:],
                                        hist[:, nh * 512:nh * 512 + 512],
                                        start=True, stop=True)
                    ws = ()
                    if nh == 0:
                        ws = (wv, "ones", f"uw2ev{KT - 2}_{t}", e2) \
                            if which == "b1" else (wv, f"b1u{m}")
                    op("pe", f, waits=ws,
                       event=f"{which}mm{t}" if nh == 1 else None,
                       wr=("PY" if nh == 0 else "PY2",))
                for nh in range(2):
                    def g(e, which=which, nh=nh, boff=boff):
                        bank = PY if nh == 0 else PY2
                        sc = -LR if which == "b1" else -LR * C0
                        return e.scalar_tensor_tensor(
                            bvec[0:1, boff + nh * 512:boff + nh * 512 + 512],
                            bank[0:1, :], sc,
                            bvec[0:1, boff + nh * 512:boff + nh * 512 + 512],
                            ALU.mult, ALU.add)
                    op("ve", g, waits=(f"{which}mm{t}",),
                       event=f"{which}u{m}" if nh == 1 else None,
                       rd=("PY" if nh == 0 else "PY2",))
            op("gp", lambda e: e.tensor_copy(W2b[:], W2[:]),
               waits=(f"w2u{m}",), event=f"w2b{m}")
            sch.alias(f"u_done{m}", e1)
            sch.alias(f"y_free{m + 1}", f"b2u{m}")
            sch.alias(f"d_free{m + 1}", f"uw2ev{KT - 1}_{t}")

        Mlast = n_chunks - 1
        Tlast = n_chunks * NSTEPS - 1

        def ph3a(c):
            if c == 0:
                op("gp", lambda e: e.memset(fl64[:], 1e-12),
                   waits=(f"p7{Tlast}",), event="fl64b")
                sch.events["fl64"] = sch.events["fl64b"]
            xw = (f"pxdma{c - 1}",) if c >= 1 \
                else (f"cast_kr{Mlast}", f"w2b{Mlast}")
            op("sy_d", lambda e: e.dma_start(out=xrow[:], in_=xo[c]),
               waits=xw, event=f"ldx3{c}", chan="ldk")
            lnw = [f"ldx3{c}"]
            if c >= 1:
                lnw.append(f"pxdma{c - 1}")
            else:
                lnw += [f"spraw{Mlast}", f"a0t{Mlast}_tp1"]
            ln_emit(f"a{c}", xrow, A0row, tuple(lnw))
            xnw = [f"lna{c}"]
            if c >= 1:
                xnw.append(f"pxdma{c - 1}")
            xn_emit(f"a{c}", xrow, A0row, tuple(xnw))
            if DBG_XN:
                op("sy_d", lambda e, c=c: e.dma_start(out=dxn[c],
                                                      in_=A0row[:]),
                   waits=(f"xna{c}",) + ((f"pxdma{c - 1}",) if c >= 1
                                         else ()),
                   event=f"dxn{c}", chan="cdh")
            if c >= 1:
                qw_acc = (f"xqt{c - 1}_tp1",)
                qw_se = (f"xna{c}", f"qdqx{c - 1}")
            else:
                qw_acc = (f"wqt{NWU - 1}_tp_last",) if NWU else ()
                qw_se = (f"xna{c}",) + ((f"qdqw{NWU - 1}",) if NWU else ())
            qdq_emit(f"x{c}", A0row[:], D, S0row[:], SProw[:],
                     w_se=qw_se,
                     w_ve=(f"xna{c}", f"h1mm{c - 1}" if c >= 1 else None),
                     w_acc=qw_acc)
            if DBG_XN:
                op("sy_d", lambda e, c=c: e.dma_start(out=dxq[c],
                                                      in_=accb[:]),
                   waits=(f"qdqx{c}", f"dxn{c}"), event=f"dxq{c}",
                   chan="cdh")
            tw = [f"xna{c}"]
            tw.append(f"a1t{c - 1}_ev1_0" if c >= 1 else f"pr_free{Tlast}")
            if DBG_XN and c >= 1:
                tw.append(f"tapa{c - 1}")
            tp_set(A0row, [H0row], f"xnt3{c}_", tuple(tw))
            for q in range(KT):
                def f(e, q=q):
                    bank = PR if q < 4 else PR2
                    return e.transpose(
                        bank[:, (q % 4) * 128:(q % 4) * 128 + 128],
                        accb[:, q * 128:(q + 1) * 128], idf[:])
                ws = (f"qdqx{c}", f"xnt3{c}_ev1_0") if q == 0 else ()
                op("pe", f, waits=ws,
                   event=(f"xqt{c}_tp0" if q == 3 else
                          f"xqt{c}_tp1" if q == 7 else None),
                   wr=("PR" if q < 4 else "PR2",))
            for half, bank in ((0, PR), (1, PR2)):
                def fe(e, half=half, bank=bank):
                    return e.tensor_copy(
                        vrow[:, half * 512:half * 512 + 512], bank[:, 0:512])
                hw = [f"xqt{c}_tp{half}"]
                if half == 0:
                    hw.append(f"xqdma{c - 1}" if c >= 1 else
                              (f"wqs{NWU - 1}" if NWU else None))
                    if c == 0:
                        hw.append(f"dy0{Mlast}")
                op("ve", fe, waits=tuple(w for w in hw if w),
                   event=f"xqte{c}_{half}",
                   rd=("PR" if half == 0 else "PR2",))
            op("sy_d", lambda e: e.dma_start(out=xqT_h[c], in_=vrow[:]),
               waits=(f"xqte{c}_1",), event=f"xqdma{c}", chan="cdy")
            for kt in range(KT):
                stg = wst if kt % 2 == 0 else vrow
                dw = []
                if kt >= 1:
                    dw.append(f"qmm3{c}_{kt - 1}")
                    if kt == 1:
                        dw.append(f"xqdma{c}")
                        if c >= 1:
                            dw.append(f"a1m{c - 1}")
                        if DBG_XN and c >= 1:
                            dw.append(f"tapd{c - 1}")
                elif c >= 1:
                    dw.append(f"qmm3{c - 1}_{KT - 1}")
                    dw.append(f"a1m{c - 1}")
                else:
                    dw.append(f"cast_kt{Mlast}")
                    dw.append(f"w2b{Mlast}")
                    if NWU:
                        dw += [f"wq_rd{NWU - 1}", f"wq_sg{NWU - 1}"]
                def fd(e, kt=kt, stg=stg):
                    return e.dma_start(out=stg[:],
                                       in_=wq_i[kt * 128:(kt + 1) * 128, :])
                op("sy_d", fd, waits=tuple(dw), event=f"wqd3{c}_{kt}",
                   chan="ca")
                for nh in range(2):
                    def f(e, kt=kt, nh=nh, stg=stg):
                        bank = PY if nh == 0 else PY2
                        return e.matmul(
                            bank[:, 0:512], rr(H0row[:])[:, kt, :],
                            stg[:, nh * 512:nh * 512 + 512],
                            start=(kt == 0), stop=(kt == KT - 1))
                    ws = [f"wqd3{c}_{kt}"]
                    if kt == 0 and nh == 0:
                        ws.append(f"xnt3{c}_ev1_0")
                        ws.append(f"h1ev{c - 1}" if c >= 1 else f"b2u{Mlast}")
                    op("pe", f, waits=tuple(ws) if nh == 0 else (),
                       event=f"qmm3{c}_{kt}" if nh == 1 else None,
                       wr=("PY" if nh == 0 else "PY2",))
            for nh, bank in ((0, PY), (1, PY2)):
                def f(e, nh=nh, bank=bank):
                    return e.tensor_copy(S0row[:, nh * 512:nh * 512 + 512],
                                         bank[:, 0:512])
                ws = [f"qmm3{c}_{KT - 1}", f"qabx{c}"]
                if c >= 1:
                    ws.append(f"qt3{c - 1}_tp1")
                op("ve", f, waits=tuple(ws) if nh == 0 else (),
                   event=f"qev3{c}" if nh == 1 else None,
                   rd=("PY" if nh == 0 else "PY2",))
            if DBG_XN:
                op("sy_d", lambda e, c=c: e.dma_start(out=dq[c],
                                                      in_=S0row[:]),
                   waits=(f"qev3{c}", f"dxq{c}"), event=f"dq{c}",
                   chan="cdh")
            tp_set(S0row, [SProw], f"qt3{c}_", (f"qev3{c}", f"xqte{c}_1"))
            for kt in range(KT):
                for nh in range(2):
                    def f(e, kt=kt, nh=nh):
                        bank = PY if nh == 0 else PY2
                        return e.matmul(
                            bank[:, 0:512], rr(SProw[:])[:, kt, :],
                            W1[:, kt * D + nh * 512:kt * D + nh * 512 + 512],
                            start=(kt == 0), stop=False)
                    ws = ()
                    if kt == 0 and nh == 0:
                        ws = (f"qt3{c}_ev1_0", f"qev3{c}", f"w1u{Mlast}")
                    op("pe", f, waits=ws, wr=("PY" if nh == 0 else "PY2",))
            for nh in range(2):
                def f(e, nh=nh):
                    bank = PY if nh == 0 else PY2
                    return e.matmul(bank[:, 0:512], onesf[:],
                                    bvec[0:1, nh * 512:nh * 512 + 512],
                                    start=False, stop=True)
                op("pe", f, waits=(f"b1u{Mlast}",) if nh == 0 else (),
                   event=f"h1mm{c}" if nh == 1 else None,
                   wr=("PY" if nh == 0 else "PY2",))
            h1t = vrow[:]
            sgt = wst
            a1t = H0row[:]
            for nh, bank in ((0, PY), (1, PY2)):
                def f(e, nh=nh, bank=bank):
                    return e.tensor_copy(h1t[:, nh * 512:nh * 512 + 512],
                                         bank[:, 0:512])
                op("ve", f, waits=(f"h1mm{c}",) if nh == 0 else (),
                   event=f"h1ev{c}" if nh == 1 else None,
                   rd=("PY" if nh == 0 else "PY2",))
            if DBG_XN:
                op("sy_d", lambda e, c=c: e.dma_start(out=dh1[c],
                                                      in_=h1t[:]),
                   waits=(f"h1ev{c}", f"dq{c}"), event=f"tapd{c}",
                   chan="cdh")
            op("se", lambda e: e.activation(sgt[:], h1t[:], AF.Sigmoid),
               waits=(f"h1ev{c}",), event=f"sg3{c}")
            op("ve", lambda e: e.tensor_mul(a1t[:], h1t[:], sgt[:]),
               waits=(f"sg3{c}",), event=f"a1m{c}")
            if DBG_XN:
                op("sy_d", lambda e, c=c: e.dma_start(out=da1[c],
                                                      in_=a1t[:]),
                   waits=(f"a1m{c}", f"tapd{c}"), event=f"tapa{c}",
                   chan="cdh")
            a1T = W2Tb[:].bitcast(F32)[:, 3072:4096]
            tp_set(a1t, [a1T], f"a1t{c}_", (f"a1m{c}",))
            for kt in range(KT):
                for nh in range(2):
                    def f(e, kt=kt, nh=nh):
                        bank = PD if nh == 0 else PD2
                        return e.matmul(
                            bank[:, 0:512], rr(a1T)[:, kt, :],
                            W2[:, kt * D + nh * 512:kt * D + nh * 512 + 512],
                            start=(kt == 0), stop=False)
                    ws = ()
                    if kt == 0 and nh == 0:
                        ws = (f"a1t{c}_ev1_0", f"w2u{Mlast}")
                        ws += ((f"pxadd{c - 1}",) if c >= 1 else
                               (f"uw2ev{KT - 1}_{Mlast}",))
                    op("pe", f, waits=ws, wr=("PD" if nh == 0 else "PD2",))
            for nh in range(2):
                def f(e, nh=nh):
                    bank = PD if nh == 0 else PD2
                    return e.matmul(bank[:, 0:512], onesf[:],
                                    bvec[0:1, D + nh * 512:D + nh * 512 + 512],
                                    start=False, stop=True)
                op("pe", f, waits=(f"b2u{Mlast}",) if nh == 0 else (),
                   event=f"atmm{c}" if nh == 1 else None,
                   wr=("PD" if nh == 0 else "PD2",))
            for nh, bank in ((0, PD), (1, PD2)):
                def f(e, nh=nh, bank=bank):
                    return e.tensor_tensor(
                        A0row[:, nh * 512:nh * 512 + 512], bank[:, 0:512],
                        xrow[:, nh * 512:nh * 512 + 512], ALU.add)
                ws = (f"atmm{c}", f"xnt3{c}_tp1", f"qsgx{c}") \
                    + ((f"dxn{c}", f"dxq{c}", f"dq{c}", f"tapa{c}")
                       if DBG_XN else ()) \
                    if nh == 0 else ()
                op("ve", f, waits=ws,
                   event=f"pxadd{c}" if nh == 1 else None,
                   rd=("PD" if nh == 0 else "PD2",))
            if DBG_XN:
                op("sy_d", lambda e, c=c: e.dma_start(out=dpx[c],
                                                      in_=A0row[:]),
                   waits=(f"pxadd{c}", f"tapa{c}"), event=f"tapp{c}",
                   chan="cdh")
            op("sy_d", lambda e: e.dma_start(out=px_h[c], in_=A0row[:]),
               waits=(f"pxadd{c}",), event=f"pxdma{c}", chan="cdh")

        def ph3b():
            gf = W2b[:].bitcast(F32)
            uf = W2Tb[:].bitcast(F32)
            prevpair = None
            for s in range(NSL):
                nt = SLW[s] // 128
                c0 = SLO[s] // 128
                for which, wsrc, zf in (("g", wgT_h, gf), ("u", wuT_h, uf)):
                    for kt in range(KT):
                        def fd(e, wsrc=wsrc, kt=kt, c0=c0, zf=zf, nt=nt):
                            return e.dma_start(
                                out=zf[:, kt * 512:kt * 512 + nt * 128]
                                .rearrange("p (c x) -> p c x", x=128),
                                in_=wsrc[kt, c0:c0 + nt]
                                .rearrange("c p x -> p c x"))
                        wv = ()
                        if kt == 0 and which == "g":
                            wv = ((f"gumm{s - 1}_{n_own - 1}",) if s >= 1
                                  else (f"w2b{Mlast}",
                                        f"pxadd{n_own - 1}",
                                        f"qmm3{n_own - 1}_{KT - 1}"))
                        elif kt == 0 and which == "u":
                            if s == 0:
                                wv = (f"atmm{n_own - 1}",)
                        op("sy_d", fd, waits=wv,
                           event=f"wsl{s}_{which}{kt}", chan="ldv")
                for c in range(n_own):
                    tb = f"{s}_{c}"
                    wsl = SLW[s]
                    bw = [f"xqdma{c}", f"xqdma{n_own - 1}"]
                    if prevpair:
                        bw.append(f"gumm{prevpair}")
                    else:
                        bw.append(f"atmm{n_own - 1}")
                        bw.append(f"pxdma{n_own - 1}")
                    op("sy_d", lambda e, c=c: e.dma_start(out=A0row[:],
                                                          in_=xqT_h[c]),
                       waits=tuple(bw), event=f"bxq{tb}", chan="cdy")
                    for which, bank, bname, zf in (("g", PU, "PU", gf),
                                                   ("u", PC, "PC", uf)):
                        for kt in range(KT):
                            def f(e, kt=kt, bank=bank, zf=zf, nt=nt,
                                  wsl=wsl):
                                return e.matmul(
                                    bank[:, 0:wsl],
                                    rr(A0row[:])[:, kt, :],
                                    zf[:, kt * 512:kt * 512 + nt * 128],
                                    start=(kt == 0), stop=(kt == KT - 1))
                            ws = ()
                            if kt == 0 and which == "g":
                                ws = [f"bxq{tb}", f"wsl{s}_u{KT - 1}"]
                                if prevpair:
                                    ws.append(f"guev{prevpair}")
                                else:
                                    ws.append(f"atmm{n_own - 1}")
                                ws = tuple(ws)
                            op("pe", f, waits=ws,
                               event=(f"gumm{tb}" if which == "u"
                                      and kt == KT - 1 else None),
                               wr=(bname,))
                    for which, bank, bname, dst, off in (
                            ("g", PU, "PU", S0row, 0),
                            ("u", PC, "PC", SProw, 0)):
                        def f(e, bank=bank, dst=dst, off=off, wsl=wsl):
                            return e.tensor_copy(dst[:, off:off + wsl],
                                                 bank[:, 0:wsl])
                        ws = [f"gumm{tb}"]
                        if which == "g" and prevpair:
                            ws.append(f"hmul{prevpair}")
                            ws.append(f"qabh{prevpair}")
                            ws.append(f"hqe{prevpair}")
                            if DBG_XN:
                                ws.append(f"tapg{prevpair}")
                        op("ve", f, waits=tuple(ws) if which == "g" else (),
                           event=f"guev{tb}" if which == "u" else None,
                           rd=(bname,))
                    op("se", lambda e, wsl=wsl: e.activation(
                        SProw[:, 512:512 + wsl], S0row[:, 0:wsl],
                        AF.Sigmoid),
                       waits=(f"guev{tb}",), event=f"sg_{tb}")
                    def f1(e, wsl=wsl):
                        return e.tensor_mul(S0row[:, 0:wsl],
                                            S0row[:, 0:wsl],
                                            SProw[:, 512:512 + wsl])
                    op("ve", f1, waits=(f"sg_{tb}",))
                    def f2(e, wsl=wsl):
                        return e.tensor_mul(S0row[:, 0:wsl],
                                            S0row[:, 0:wsl],
                                            SProw[:, 0:wsl])
                    op("ve", f2, event=f"hmul{tb}")
                    if prevpair:
                        w_acc = (f"hqt{prevpair}_tp_last",)
                    else:
                        w_acc = (f"xqt{n_own - 1}_tp1",)
                    qdq_emit(f"h{tb}", S0row[:, 0:wsl], wsl,
                             H0row[:, 0:512],
                             H0row[:, 512:768],
                             w_se=(f"hmul{tb}",), w_ve=(f"hmul{tb}",),
                             w_acc=w_acc)
                    for q in range(nt):
                        def f(e, q=q):
                            return e.transpose(
                                PR[:, q * 128:q * 128 + 128],
                                accb[:, q * 128:(q + 1) * 128], idf[:])
                        ws = ()
                        if q == 0:
                            ws = [f"qdqh{tb}"]
                            if prevpair:
                                ws.append(f"hqe{prevpair}")
                            else:
                                ws.append(f"a1t{n_own - 1}_ev1_0")
                            ws = tuple(ws)
                        op("pe", f, waits=ws,
                           event=f"hqt{tb}_tp_last" if q == nt - 1 else None,
                           wr=("PR",))
                    def fe(e, wsl=wsl):
                        return e.tensor_copy(wtb[:, 0:wsl], PR[:, 0:wsl])
                    ew = [f"hqt{tb}_tp_last"]
                    if prevpair:
                        ew.append(f"hqdma{prevpair}")
                    else:
                        ew.append(f"xqdma{n_own - 1}")
                    op("ve", fe, waits=tuple(ew), event=f"hqe{tb}",
                       rd=("PR",))
                    def fd(e, c=c, s=s, wsl=wsl):
                        return e.dma_start(
                            out=hq_h[c, :, SLO[s]:SLO[s] + wsl],
                            in_=wtb[:, 0:wsl])
                    op("sy_d", fd, waits=(f"hqe{tb}",),
                       event=f"hqdma{tb}", chan="ca")
                    if DBG_XN:
                        def fdd(e, c=c, s=s, wsl=wsl):
                            return e.dma_start(
                                out=dhq_o[c, :, SLO[s]:SLO[s] + wsl],
                                in_=wtb[:, 0:wsl])
                        op("sy_d", fdd, waits=(f"hqdma{tb}",),
                           event=f"taph{tb}", chan="ca")
                        def fdg(e, c=c, s=s, wsl=wsl):
                            return e.dma_start(
                                out=dgt[c, :, SLO[s]:SLO[s] + wsl],
                                in_=S0row[:, 0:wsl])
                        op("sy_d", fdg, waits=(f"hmul{tb}", f"taph{tb}"),
                           event=f"tapg{tb}", chan="cdh")
                    prevpair = tb
            return prevpair

        def ph3c(c, lastpair):
            for s in range(NSL):
                nkb = SLW[s] // 128
                kb0 = SLO[s] // 128
                z = s % 2
                dw = []
                if s >= 2:
                    dw.append(f"dnmm{c}_{s - 2}")
                elif c >= 1:
                    dw.append(f"dnmm{c - 1}_{NSL - 2 + s}")
                else:
                    dw.append(f"pd_scan_done")
                    dw.append(f"hqdma{NSL - 1}_{n_own - 1}")
                    if z == 1:
                        dw.append(f"xna{n_own - 1}")
                        dw.append(f"atmm{n_own - 1}")
                def fd(e, nkb=nkb, kb0=kb0, z=z):
                    return e.dma_start(
                        out=W2Tb[:, z * 4096:z * 4096 + nkb * 1024]
                        .rearrange("p (kb kt y) -> p kb kt y", kt=KT, y=128),
                        in_=wdT_h[kb0:kb0 + nkb]
                        .rearrange("kb kt p y -> p kb kt y"))
                op("sy_d", fd, waits=tuple(dw), event=f"wdl{c}_{s}",
                   chan="ldv" if z == 0 else "ldk")
                hw = [f"hqdma{s}_{c}"]
                if s >= 2:
                    hw.append(f"dnmm{c}_{s - 2}")
                elif c >= 1:
                    hw.append(f"dnmm{c - 1}_{NSL - 2 + s}")
                else:
                    hw.append(f"gumm{lastpair}")
                    if z == 1:
                        hw.append(f"hqdma{NSL - 1}_{n_own - 1}")
                def fh(e, c=c, s=s, z=z):
                    return e.dma_start(
                        out=hDH[:, z * 512:z * 512 + SLW[s]],
                        in_=hq_h[c, :, SLO[s]:SLO[s] + SLW[s]])
                op("sy_d", fh, waits=tuple(hw), event=f"hql{c}_{s}",
                   chan="cdy" if z == 0 else "ca")
                if DBG_XN:
                    def ftz(e, c=c, s=s, z=z):
                        return e.dma_start(out=dwz[c, s],
                                           in_=W2Tb[:, z * 4096:
                                                    z * 4096 + 4096])
                    op("sy_d", ftz, waits=(f"wdl{c}_{s}", f"hql{c}_{s}"),
                       event=f"tapz{c}_{s}", chan="cst")
                    def fth(e, c=c, s=s, z=z):
                        return e.dma_start(out=dhd[c, s],
                                           in_=hDH[:, z * 512:z * 512 + 512])
                    op("sy_d", fth, waits=(f"tapz{c}_{s}",),
                       event=f"taphd{c}_{s}", chan="cst")
                for kbl in range(nkb):
                    for nh in range(2):
                        def f(e, kbl=kbl, nh=nh, z=z, s=s, nkb=nkb):
                            bank = PD if nh == 0 else PD2
                            return e.matmul(
                                bank[:, 0:512],
                                hDH[:, z * 512 + kbl * 128:
                                    z * 512 + kbl * 128 + 128],
                                W2Tb[:, z * 4096 + kbl * 1024 + nh * 512:
                                     z * 4096 + kbl * 1024 + nh * 512 + 512],
                                start=(s == 0 and kbl == 0),
                                stop=(s == NSL - 1 and kbl == nkb - 1))
                        ws = []
                        if kbl == 0 and nh == 0:
                            ws = [f"wdl{c}_{s}", f"hql{c}_{s}"]
                            if DBG_XN:
                                ws.append(f"taphd{c}_{s}")
                            if s == 0:
                                ws.append(f"outev{c - 1}" if c >= 1
                                          else f"pxadd{n_own - 1}")
                        op("pe", f, waits=tuple(ws),
                           event=(f"dnmm{c}_{s}" if nh == 1
                                  and kbl == nkb - 1 else None),
                           wr=("PD" if nh == 0 else "PD2",))
            pw = [f"pxdma{c}", f"pxdma{n_own - 1}"]
            if c >= 1:
                pw.append(f"outev{c - 1}")
            else:
                pw.append(f"hqdma{NSL - 1}_{n_own - 1}")
            op("sy_d", lambda e, c=c: e.dma_start(out=S0row[:], in_=px_h[c]),
               waits=tuple(pw), event=f"pxl3{c}", chan="cdh")
            for nh, bank in ((0, PD), (1, PD2)):
                def f(e, nh=nh, bank=bank):
                    return e.tensor_tensor(
                        H0row[:, nh * 512:nh * 512 + 512], bank[:, 0:512],
                        S0row[:, nh * 512:nh * 512 + 512], ALU.add)
                ws = [f"dnmm{c}_{NSL - 1}", f"pxl3{c}"]
                if c >= 1:
                    ws.append(f"outdma{c - 1}")
                op("ve", f, waits=tuple(ws) if nh == 0 else (),
                   event=f"outev{c}" if nh == 1 else None,
                   rd=("PD" if nh == 0 else "PD2",))
            op("sy_d", lambda e, c=c: e.dma_start(out=outp[c], in_=H0row[:]),
               waits=(f"outev{c}",), event=f"outdma{c}", chan="cst")

        if not full:
            init_z()
            init_masters()
        if limit == "loads":
            loads(0)
            op("sy_d", lambda e: e.dma_start(
                out=w1f_o.rearrange("(k p) n -> p k n", p=128), in_=rr(W1[:])),
               waits=("cast_kr0", "vadj0"), event="st_w1", chan="cst")
        elif limit == "bases":
            loads(0)
            bases(0)
            bw = {0: ("g1u0",), 1: ("g1x0_1",), 2: ("sp00",),
                  3: ("dy0t0_ev1_0",), 4: ("dh00",),
                  }.get(BSTAGE, ("h0pt0_ev1_0", "da00", "g1x0_1"))
            op("sy_d", lambda e: e.dma_start(
                out=w1f_o[0:128, :], in_=H0pT[:]),
               waits=bw, event="st_w1", chan="cst")
        elif limit == "step0":
            loads(0)
            bases(0)
            step(0, 0)
            op("sy_d", lambda e: e.dma_start(
                out=w1f_o[0:128, :], in_=H0pT[:]),
               waits=("dma_dh0", "dma_a0", "dma_dy0"), event="st_w1", chan="cst")
        elif full:
            init_wkv()
            NWU_PER = -(-NWU // n_chunks)
            for m in range(n_chunks):
                pp(m)
                for ui in range(m * NWU_PER, min(NWU, (m + 1) * NWU_PER)):
                    wqdq_unit(ui, m)
            zw = [f"ppt{Mlast}_tp1", f"ppwt{Mlast}", f"ppwk{Mlast}",
                  f"ppwv{Mlast}", f"ppvm{Mlast}", f"xnp{Mlast}"]
            if NWU:
                zw += [f"wqs{NWU - 1}", f"wqt{NWU - 1}_tp_last",
                       f"wq_rd{NWU - 1}", f"wq_sg{NWU - 1}",
                       f"qdqw{NWU - 1}"]
            init_z(tuple(zw))
            init_masters((f"ppvm{Mlast}",))
            for m in range(n_chunks):
                loads(m)
                bases(m)
                for r in range(NSTEPS):
                    step(m, r)
                updates(m)
            sch.alias("pd_scan_done", f"pd{Tlast}")
            for c in range(n_own):
                ph3a(c)
            lastpair = ph3b()
            for c in range(n_own):
                ph3c(c, lastpair)
        else:
            for m in range(n_chunks):
                loads(m)
                bases(m)
                for r in range(NSTEPS):
                    step(m, r)
                updates(m)
            mlast = n_chunks - 1
            op("sy_d", lambda e: e.dma_start(
                out=w1f_o.rearrange("(k p) n -> p k n", p=128), in_=rr(W1[:])),
               waits=(f"w1u{mlast}",), event="st_w1", chan="cst")
            op("sy_d", lambda e: e.dma_start(
                out=w2f_o.rearrange("(k p) n -> p k n", p=128), in_=rr(W2[:])),
               waits=(f"w2u{mlast}",), event="st_w2", chan="cst")
            op("sy_d", lambda e: e.dma_start(out=bf_o[0:1, :], in_=bvec[0:1, 0:D]),
               waits=(f"b1u{mlast}",), event="st_b1", chan="cst")
            op("sy_d", lambda e: e.dma_start(out=bf_o[1:2, :], in_=bvec[0:1, D:2 * D]),
               waits=(f"b2u{mlast}",), event="st_b2", chan="cst")

        with (
            nc.semaphore("pe_s") as pe_s,
            nc.semaphore("ve_s") as ve_s,
            nc.semaphore("se_s") as se_s,
            nc.semaphore("gp_s") as gp_s,
            nc.semaphore("s_ldk") as s_ldk,
            nc.semaphore("s_ldv") as s_ldv,
            nc.semaphore("s_cdh") as s_cdh,
            nc.semaphore("s_cdy") as s_cdy,
            nc.semaphore("s_ca") as s_ca,
            nc.semaphore("s_cst") as s_cst,
            nc.Block() as block,
        ):
            sems = {"pe": pe_s, "ve": ve_s, "se": se_s, "gp": gp_s,
                    "sy:ldk": s_ldk, "sy:ldv": s_ldv, "sy:cdh": s_cdh,
                    "sy:cdy": s_cdy, "sy:ca": s_ca, "sy:cst": s_cst}

            @block.tensor
            def _(e):
                sch.emit("pe", e, sems)

            @block.vector
            def _(e):
                sch.emit("ve", e, sems)

            @block.scalar
            def _(e):
                sch.emit("se", e, sems)

            @block.gpsimd
            def _(e):
                sch.emit("gp", e, sems)

            @block.sync
            def _(e):
                sch.emit("sy_d", e, sems)
                for c in sch.DMACH:
                    e.wait_ge(sems[f"sy:{c}"], sch.counts[f"sy:{c}"])
    return nc


def kernel(**inputs):
    x = np.asarray(inputs["L0_attn_x"], np.float32)
    WQ = np.asarray(inputs["L0_attn_W_Q"], np.float32)
    WKV = np.asarray(inputs["L0_attn_W_KV"], np.float32)
    W1 = np.asarray(inputs["L0_attn_W1"], np.float32)
    b1 = np.asarray(inputs["L0_attn_b1"], np.float32).reshape(1, D)
    W2 = np.asarray(inputs["L0_attn_W2"], np.float32)
    b2 = np.asarray(inputs["L0_attn_b2"], np.float32).reshape(1, D)
    wg = np.asarray(inputs["L0_ffn_w_gate"], np.float32)
    wu = np.asarray(inputs["L0_ffn_w_up"], np.float32)
    wd = np.asarray(inputs["L0_ffn_w_down"], np.float32)

    Bq, Sq = x.shape[0], x.shape[1]
    n_chunks = Bq * Sq // 128
    n_own = n_chunks // 8
    X2 = np.ascontiguousarray(x.transpose(1, 0, 2)).reshape(Bq * Sq, D)
    xs = np.ascontiguousarray(X2.reshape(n_chunks, 128, D))

    import ml_dtypes
    ident = np.eye(128, dtype=np.float32)
    jj = np.arange(128)
    masku = ((jj[:, None] // 8) < (jj[None, :] // 8)).astype(ml_dtypes.bfloat16)

    nc = build_scan_program(n_chunks, mode="full", n_own=n_own)
    base = {
        "w1_0": W1, "w2_0": W2, "b1_0": b1, "b2_0": b2,
        "ident": ident, "masku": masku,
        "xs": xs, "wkv": WKV, "wq": WQ, "wg": wg, "wu": wu, "wd": wd,
    }
    in_maps = []
    for core in range(8):
        m = dict(base)
        m["xo"] = np.ascontiguousarray(xs[core * n_own:(core + 1) * n_own])
        in_maps.append(m)
    res = run_bass_kernel_spmd(nc, in_maps, core_ids=list(range(8)))
    out2 = np.empty((Bq * Sq, D), np.float32)
    for core in range(8):
        o = np.asarray(res.results[core]["outp"], np.float32)
        out2[core * n_own * 128:(core + 1) * n_own * 128] = \
            o.reshape(n_own * 128, D)
    return np.ascontiguousarray(
        out2.reshape(Sq, Bq, D).transpose(1, 0, 2))
